# version 45
# baseline (speedup 1.0000x reference)
import numpy as np
import ml_dtypes
import concourse.bass as bass
import concourse.mybir as mybir
from concourse.bass_utils import run_bass_kernel_spmd

F32 = mybir.dt.float32
BF16 = mybir.dt.bfloat16
AF = mybir.ActivationFunctionType
ALU = mybir.AluOpType
AX = mybir.AxisListType

NCORE = 8
NG = 8
TC = 4
DCN = 8
NH = 4
KC = 4
EPS = 1e-5

ENGS = ("sync", "tensor", "vector", "scalar", "gpsimd")


class Prog:

    def __init__(self, sems):
        self.sems = sems
        self.cnt = {k: 0 for k in sems}
        self.ops = {e: [] for e in ENGS}
        self.tags = {}
        self.phase = "pre"
        self.last_wait = {e: {} for e in ENGS}

    def op(self, eng, fn, after=(), inc=None, inc_amt=1, tag=None):
        self.tags.setdefault(eng, []).append(tag or self.phase)
        waits = []
        for ev in after:
            if ev is None:
                continue
            s, v = ev
            lw = self.last_wait[eng].get(s, -1)
            if v > lw:
                self.last_wait[eng][s] = v
                waits.append((self.sems[s], v))
        ev_out = None
        handle = None
        if inc is not None:
            self.cnt[inc] += inc_amt
            ev_out = (inc, self.cnt[inc])
            handle = self.sems[inc]
        self.ops[eng].append((fn, waits, handle, inc_amt))
        return ev_out

    def emit(self, block):
        def make(eng_name):
            oplist = self.ops[eng_name]

            def body(e):
                for (fn, waits, inc_h, inc_amt) in oplist:
                    for (h, v) in waits:
                        e.wait_ge(h, v)
                    inst = fn(e)
                    if inc_h is not None:
                        inst.then_inc(inc_h, inc_amt)
            return body

        block.sync(make("sync"))
        block.tensor(make("tensor"))
        block.vector(make("vector"))
        block.scalar(make("scalar"))
        block.gpsimd(make("gpsimd"))


def build_nc():
    nc = bass.Bass("TRN2", target_bir_lowering=False)

    xin = nc.dram_tensor("xin", [4096, 1024], F32, kind="ExternalInput")
    wqs_h = nc.dram_tensor("wqs", [128, 8192], F32, kind="ExternalInput")
    wks_h = nc.dram_tensor("wks", [128, 8192], F32, kind="ExternalInput")
    wvs_h = nc.dram_tensor("wvs", [128, 8192], F32, kind="ExternalInput")
    bqc_h = nc.dram_tensor("bqc", [128, 8], F32, kind="ExternalInput")
    bkc_h = nc.dram_tensor("bkc", [128, 8], F32, kind="ExternalInput")
    bvb_h = nc.dram_tensor("bvb", [128, 1024], F32, kind="ExternalInput")
    fk_h = nc.dram_tensor("fk", [1024, 1024], F32, kind="ExternalInput")
    fbias_h = nc.dram_tensor("fbias", [1024], F32, kind="ExternalInput")
    fa_h = nc.dram_tensor("fa", [1024, 16], F32, kind="ExternalInput")
    fb_h = nc.dram_tensor("fb", [16, 1024], F32, kind="ExternalInput")
    idb_h = nc.dram_tensor("identb", [128, 128], BF16, kind="ExternalInput")
    idf_h = nc.dram_tensor("identf", [128, 128], F32, kind="ExternalInput")
    out_h = nc.dram_tensor("out", [4096, 1024], F32, kind="ExternalOutput")

    from contextlib import ExitStack
    es = ExitStack()
    sb = lambda name, shape, dt: es.enter_context(nc.sbuf_tensor(name, shape, dt))
    ps = lambda name, shape, dt: es.enter_context(nc.psum_tensor(name, shape, dt))

    with es:
        A = [sb("A0", [128, 4096], F32), sb("A1", [128, 4096], F32),
             sb("A2", [128, 4096], F32)]
        B = sb("B", [128, 4096], F32)
        C1 = sb("C1", [128, 4096], BF16)
        C2 = sb("C2", [128, 4096], BF16)
        D = [sb("D0", [128, 4096], BF16), sb("D1", [128, 4096], BF16)]
        QF = sb("QF", [128, 1024], BF16)
        KF = sb("KF", [128, 1024], BF16)
        VA = sb("VA", [128, 1056], BF16)
        PT = sb("PT", [128, 4096], BF16)
        WQS = sb("WQS", [128, 8192], BF16)
        WKS = sb("WKS", [128, 8192], BF16)
        WVS = sb("WVS", [128, 8192], BF16)
        WFS = sb("WFS", [128, 8192], BF16)
        AAS = sb("AAS", [128, 128], BF16)
        BAS = sb("BAS", [32, 1024], BF16)
        XAS = sb("XAS", [32, 512], BF16)
        ROWB = sb("ROWB", [1, 1024], BF16)
        BQC = sb("BQC", [128, 8], F32)
        BKC = sb("BKC", [128, 8], F32)
        BVB = sb("BVB", [128, 1024], F32)
        STG = [sb("STG0", [128, 1024], F32), sb("STG1", [128, 1024], F32)]
        ONE1 = sb("ONE1", [1, 128], F32)
        ROWF = sb("ROWF", [1, 256], F32)
        SC = sb("SC", [128, 72], F32)
        DSCV = sb("DSCV", [128, 1024], BF16)
        DSCA = sb("DSCA", [128, 1024], BF16)
        IDB = sb("IDB", [128, 128], BF16)
        IDF = sb("IDF", [128, 128], F32)

        P_T = ps("P_T", [128, 1024], BF16)
        P_S = ps("P_S", [128, 2048], F32)
        P_A = ps("P_A", [128, 512], F32)
        P_B = ps("P_B", [128, 512], F32)
        P_C = ps("P_C", [128, 512], F32)

        with (
            nc.semaphore("sQ") as hQ,
            nc.semaphore("sP") as hP,
            nc.semaphore("sV") as hV,
            nc.semaphore("sA") as hA,
            nc.semaphore("sL") as hL,
            nc.Block() as block,
        ):
            P = Prog({"sQ": hQ, "sP": hP, "sV": hV, "sA": hA, "sL": hL})
            global _LAST_PROG
            _LAST_PROG = P

            ZS, SSQ1, M1, R1, NMR1 = 0, 4, 8, 12, 16
            Z2H, Z2S, SSQ2, R2, NMR2 = 20, 28, 32, 36, 40
            MCOL, EB, RR4 = 44, 45, 46
            QT2, MG, VART, LNT = 50, 51, 52, 56
            QAMX, QRS, QSC = 60, 61, 62
            MXP = 64

            def dma(dst, src):
                return lambda e, d=dst, s=src: e.dma_start(d, s)

            def mk_copy(dst, src):
                return lambda e, d=dst, s=src: e.tensor_copy(d, s)

            def mk_mm(dst, lhs, rhs, start, stop):
                return lambda e, d=dst, l=lhs, r_=rhs, st=start, sp=stop: e.matmul(
                    d, l, r_, start=st, stop=sp)

            def mk_tr(dst, src, ident):
                return lambda e, d=dst, s=src, i=ident: e.transpose(d, s, i)

            def mk_ts(dst, src, s1, s2, op0, op1=ALU.bypass, accum=None):
                if s2 is None:
                    return lambda e, d=dst, s=src, a=s1, o=op0: e.tensor_scalar(
                        d, s, a, None, o)
                return lambda e, d=dst, s=src, a=s1, b=s2, o0=op0, o1=op1, ac=accum: \
                    e.tensor_scalar(d, s, a, b, o0, o1, accum_out=ac)

            def mk_stt(dst, in0, scal, in1, op0, op1, accum=None):
                return lambda e, d=dst, i0=in0, sc=scal, i1=in1, o0=op0, o1=op1, ac=accum: \
                    e.scalar_tensor_tensor(d, i0, sc, i1, o0, o1, accum_out=ac)

            def mk_tt(dst, in0, in1, op):
                return lambda e, d=dst, i0=in0, i1=in1, o=op: e.tensor_tensor(d, i0, i1, o)

            def mk_red(dst, src, op, absval=False):
                return lambda e, d=dst, s=src, o=op, av=absval: e.tensor_reduce(
                    d, s, AX.X, o, apply_absolute_value=av)

            def mk_act(dst, src, func, bias=0.0, scale=1.0, accum=None):
                return lambda e, d=dst, s=src, f=func, b=bias, sc=scale, ac=accum: \
                    e.activation(d, s, f, bias=b, scale=sc, accum_out=ac)

            ev_idb = P.op("sync", dma(IDB[:, :], idb_h[:, :]), inc="sQ", inc_amt=16)
            ev_idf = P.op("sync", dma(IDF[:, :], idf_h[:, :]), inc="sQ", inc_amt=16)

            def xload_ap(g):
                p, r = g // 4, g % 4
                base = p * 2048 + r
                return xin[base: base + 2045: 4, :].rearrange("(tc p) c -> p tc c", p=128)

            def xstore_ap(g):
                p, r = g // 4, g % 4
                base = p * 2048 + r
                return out_h[base: base + 2045: 4, :].rearrange("(tc p) c -> p tc c", p=128)

            def t3(t):
                return t[:, :].rearrange("p (tc c) -> p tc c", c=1024)

            ev_load = [None] * NG
            ev_load[0] = P.op("sync", dma(t3(A[0]), xload_ap(0)), inc="sQ", inc_amt=16)

            ev_one1 = P.op("vector", lambda e: e.memset(ONE1[:, :], 1.0), inc="sV")
            ev_eps = P.op("vector", lambda e: e.memset(SC[:, 63:64], EPS), inc="sV")
            ev_va0 = P.op("vector", lambda e: e.memset(VA[:, :], 0.0), inc="sV")
            va4 = VA[:, :].rearrange("p (kc h w) -> p kc h w", kc=4, w=66)
            ev_vones = P.op("vector", lambda e: e.memset(va4[:, :, :, 64:65], 1.0),
                            after=[ev_va0], inc="sV")
            ev_xas0 = P.op("gpsimd", lambda e: e.memset(XAS[:, :], 0.0), inc="sL")
            ev_bas0 = P.op("gpsimd", lambda e: e.memset(BAS[:, :], 0.0), inc="sL")
            ev_rb1 = P.op("gpsimd", lambda e: e.memset(ROWB[:, 0:512], 1.0), inc="sL")
            ev_xasones = P.op("sync", dma(XAS[16:17, :], ROWB[0:1, 0:512]),
                              after=[ev_xas0, ev_rb1], inc="sQ", inc_amt=16)

            P.op("sync", dma(BQC[:, :], bqc_h[:, :]), inc="sQ", inc_amt=16)
            ev_bqkc = P.op("sync", dma(BKC[:, :], bkc_h[:, :]), inc="sQ", inc_amt=16)
            ev_bvb = P.op("sync", dma(BVB[:, :], bvb_h[:, :]), inc="sQ", inc_amt=16)

            ev_fbrow = P.op("sync", dma(STG[0][0:1, :],
                                        fbias_h[:].rearrange("(o c) -> o c", o=1)),
                            inc="sQ", inc_amt=16)
            ev_rb2 = P.op("gpsimd", mk_copy(ROWB[0:1, :], STG[0][0:1, :]),
                          after=[ev_fbrow, ev_xasones], inc="sL")
            ev_basbias = P.op("sync", dma(BAS[16:17, :], ROWB[0:1, :]),
                              after=[ev_rb2, ev_bas0], inc="sQ", inc_amt=16)

            ev_fa = P.op("sync", dma(STG[1][:, 0:128].rearrange("p (dc c) -> p dc c", c=16),
                                     fa_h[:, :].rearrange("(dc p) c -> p dc c", p=128)),
                         after=[ev_bvb], inc="sQ", inc_amt=16)
            ev_aas = P.op("gpsimd", mk_copy(AAS[:, :], STG[1][:, 0:128]),
                          after=[ev_fa], inc="sL")

            ev_fb = P.op("sync", dma(STG[0][0:16, :], fb_h[:, :]),
                         after=[ev_rb2], inc="sQ", inc_amt=16)
            ev_bas = P.op("gpsimd", mk_copy(BAS[0:16, :], STG[0][0:16, :]),
                          after=[ev_fb, ev_bas0], inc="sL")

            ev_xbf0 = []
            for _tc in range(TC):
                ev_xbf0.append(P.op("gpsimd",
                                    mk_copy(C1[:, _tc * 1024:(_tc + 1) * 1024],
                                            A[0][:, _tc * 1024:(_tc + 1) * 1024]),
                                    after=[ev_load[0]], inc="sL"))
            stg_free = [ev_bas, ev_aas]
            ev_wc = {0: [None] * 8, 1: [None] * 8, 2: [None] * 8}
            pi = 0
            for r in range(4):
                for wi in range(3):
                    src_h = (wqs_h, wks_h, wvs_h)[wi]
                    dst = (WQS, WKS, WVS)[wi]
                    for half in range(2):
                        ck = r * 2 + half
                        par = pi % 2
                        pi += 1
                        evd = P.op("sync",
                                   dma(STG[par][:, :], src_h[:, ck * 1024:(ck + 1) * 1024]),
                                   after=[stg_free[par]], inc="sQ", inc_amt=16)
                        ev_wc[wi][ck] = P.op("vector",
                                             mk_copy(dst[:, ck * 1024:(ck + 1) * 1024],
                                                     STG[par][:, :]),
                                             after=[evd], inc="sV")
                        stg_free[par] = ev_wc[wi][ck]
                if r == 0:
                    ev_load[1] = P.op("sync", dma(t3(A[1]), xload_ap(1)),
                                      inc="sQ", inc_amt=16)

            QHALF, NEG8S = 69, 68
            ev_q85 = P.op("vector", lambda e: e.memset(SC[:, QHALF:QHALF + 1], 8.5),
                          inc="sV")
            a2q = [A[2][:, i * 1024:(i + 1) * 1024] for i in range(4)]
            ev_qdma = [None] * 8
            ev_qred = [None] * 8
            for qi in range(4):
                dc = qi + 4
                ev_qdma[dc] = P.op("gpsimd", dma(a2q[qi], fk_h[dc * 128:(dc + 1) * 128, :]),
                                   after=[ev_xbf0[TC - 1]] if qi == 0 else [],
                                   inc="sL", inc_amt=16)
            for qi in range(4):
                dc = qi + 4
                ev_qred[dc] = P.op("vector",
                                   mk_red(SC[:, QT2:QT2 + 1], a2q[qi], ALU.max,
                                          absval=True),
                                   after=[ev_qdma[dc]], inc="sV")
                if dc == 4:
                    ev_qred[dc] = P.op("vector",
                                       mk_copy(SC[:, QAMX:QAMX + 1], SC[:, QT2:QT2 + 1]),
                                       after=[ev_qred[dc]], inc="sV")
                else:
                    ev_qred[dc] = P.op("vector",
                                       mk_tt(SC[:, QAMX:QAMX + 1], SC[:, QAMX:QAMX + 1],
                                             SC[:, QT2:QT2 + 1], ALU.max),
                                       after=[ev_qred[dc]], inc="sV")
            qsc_state = {"ev_qscale": None}

            def emit_quant_rd2():
                for qi in range(4):
                    ev_qdma[qi] = P.op("gpsimd",
                                       dma(a2q[qi], fk_h[qi * 128:(qi + 1) * 128, :]),
                                       after=[ev_qred[qi + 4]], inc="sL", inc_amt=16)

            def emit_quant_scale():
                ph_save = P.phase
                P.phase = "quant"
                ev_amx = None
                for qi in range(4):
                    e_ = P.op("vector",
                              mk_red(SC[:, QT2:QT2 + 1], a2q[qi], ALU.max, absval=True),
                              after=[ev_qdma[qi]], inc="sV")
                    ev_amx = P.op("vector",
                                  mk_tt(SC[:, QAMX:QAMX + 1], SC[:, QAMX:QAMX + 1],
                                        SC[:, QT2:QT2 + 1], ALU.max),
                                  after=[e_, ev_amx], inc="sV")
                ev_td1 = P.op("sync", dma(ROWF[0:1, 0:128], SC[:, QAMX:QAMX + 1]),
                              after=[ev_amx, ev_bvb, ev_bqkc], inc="sQ", inc_amt=16)
                ev_rr = P.op("vector", mk_red(SC[0:1, QT2:QT2 + 1], ROWF[0:1, 0:128],
                                              ALU.max),
                             after=[ev_td1], inc="sV")
                ev_bc = P.op("vector",
                             mk_ts(ROWF[0:1, 128:256], ONE1[0:1, :],
                                   SC[0:1, QT2:QT2 + 1], None, ALU.mult),
                             after=[ev_rr, ev_one1], inc="sV")
                ev_td2 = P.op("sync", dma(SC[:, QAMX:QAMX + 1], ROWF[0:1, 128:256]),
                              after=[ev_bc], inc="sQ", inc_amt=16)
                e_ = P.op("vector", lambda e: e.reciprocal(SC[:, QRS:QRS + 1],
                                                           SC[:, QAMX:QAMX + 1]),
                          after=[ev_td2], inc="sV")
                e_ = P.op("vector", mk_ts(SC[:, QRS:QRS + 1], SC[:, QRS:QRS + 1], 7.0,
                                          None, ALU.mult),
                          after=[e_], inc="sV")
                e_ = P.op("vector", mk_ts(SC[:, QSC:QSC + 1], SC[:, QAMX:QAMX + 1],
                                          1.0 / 7.0, None, ALU.mult),
                          after=[e_], inc="sV")
                qsc_state["ev_qscale"] = P.op("vector",
                                              mk_ts(SC[:, NEG8S:NEG8S + 1],
                                                    SC[:, QSC:QSC + 1], -8.0, None,
                                                    ALU.mult),
                                              after=[e_], inc="sV")
                P.phase = ph_save

            qst = {"e3p": [None] * 8, "fin": [None] * 8, "prevV": None}
            ev_wf_s = [None] * 8
            PEEL_Q = {0: 0, 1: 1, 2: 2, 3: 3, 4: 0, 5: 1, 6: 2, 7: 3}
            PEEL_POOL = ()

            def emit_peel_stripe(dc):
                ph_save = P.phase
                P.phase = "quant"
                qi = PEEL_Q[dc]
                on_pool = dc in PEEL_POOL
                wsl = WFS[:, dc * 1024:(dc + 1) * 1024]
                aft_eu = [qsc_state["ev_qscale"], ev_qdma[dc]]
                if dc >= 4:
                    ev_qdma[dc] = P.op("gpsimd",
                                       dma(a2q[qi], fk_h[dc * 128:(dc + 1) * 128, :]),
                                       after=[qst["e3p"][dc - 4]], inc="sL", inc_amt=16)
                    aft_eu.append(ev_qdma[dc])
                if on_pool:
                    off = 0 if qi == 2 else 2048
                    B0 = B[:, off:off + 1024]
                    B1 = B[:, off + 1024:off + 2048]
                    if dc >= 4:
                        aft_eu += [qst["e3p"][dc - 4], qst["fin"][dc - 4]]
                else:
                    B0 = STG[0][:, :]
                    B1 = STG[1][:, :]
                    aft_eu += [stg_free[0], stg_free[1]]
                    if qst["prevV"] is not None:
                        pv = qst["prevV"]
                        aft_eu += [qst["e3p"][pv], qst["fin"][pv]]
                    qst["prevV"] = dc
                eu = P.op("vector",
                          mk_ts(B0, a2q[qi], SC[:, QRS:QRS + 1], 8.5,
                                ALU.mult, ALU.add),
                          after=aft_eu, inc="sV")
                seq = [B0, B1, a2q[qi], B1, a2q[qi]]
                eng = "gpsimd" if on_pool else "vector"
                sem = "sL" if on_pool else "sV"
                e2p = eu
                for si, k in enumerate((8.0, 4.0, 2.0, 1.0)):
                    if on_pool:
                        ebp = P.op(eng, mk_ts(wsl, seq[si], k, k,
                                              ALU.is_ge, ALU.mult),
                                   after=[e2p], inc=sem)
                        e2p = P.op(eng, mk_tt(seq[si + 1], seq[si], wsl,
                                              ALU.subtract),
                                   after=[ebp], inc=sem)
                    else:
                        ebp = P.op(eng, mk_ts(wsl, seq[si], k, None, ALU.is_ge),
                                   after=[e2p], inc=sem)
                        e2p = P.op(eng,
                                   mk_stt(seq[si + 1], wsl, -k, seq[si],
                                          ALU.mult, ALU.add),
                                   after=[ebp], inc=sem)
                if on_pool:
                    qst["e3p"][dc] = P.op(eng, mk_tt(B1, B0, seq[4], ALU.subtract),
                                          after=[e2p], inc=sem)
                else:
                    qst["e3p"][dc] = P.op(eng,
                                          mk_stt(B1, seq[4], -1.0, B0,
                                                 ALU.mult, ALU.add),
                                          after=[e2p], inc=sem)
                qst["fin"][dc] = P.op("vector",
                                      mk_ts(wsl, B1, SC[:, QSC:QSC + 1],
                                            SC[:, NEG8S:NEG8S + 1],
                                            ALU.mult, ALU.add),
                                      after=[qst["e3p"][dc]], inc="sV")
                ev_wf_s[dc] = qst["fin"][dc]
                P.phase = ph_save

            PSb = [P_S[:, i * 512:(i + 1) * 512] for i in range(4)]
            ev_afree = [None] * NG
            ev_store = [None] * NG
            bfree = {"PS0": None, "PS1": None, "PS2": None, "PS3": None,
                     "PA": None, "PB": None}
            pt_free = None
            ptbuf_free = [None, None]
            ps_free = None
            ps_free_b = [None] * 4
            st = [dict() for _ in range(NG)]

            def emit_xbf(g):
                Ag = A[g % 3]
                evs = []
                for tc in range(TC):
                    aft = [ev_load[g]]
                    if g >= 1:
                        aft.append(st[g - 1]["T1done"])
                    evs.append(P.op("gpsimd",
                                    mk_copy(C1[:, tc * 1024:(tc + 1) * 1024],
                                            Ag[:, tc * 1024:(tc + 1) * 1024]),
                                    after=aft, inc="sL"))
                st[g]["xbf"] = evs

            def phase_T1_QKV(g):
                P.phase = "T1QKV g%d" % g
                r = g % 4
                Ag = A[g % 3]
                nonlocal pt_free
                s = st[g]
                ev_xbf = s["xbf"]
                Dg = D[g % 2]
                ev_xTe = [None] * DCN
                stop_evs = {}

                def qkv_mm_block(dc):
                    start, stop = dc == 0, dc == DCN - 1
                    for cc in range(2):
                        aft = [ev_xTe[dc], ev_wc[0][r * 2 + (1 if dc >= 4 else 0)]]
                        if start:
                            aft += [bfree["PS%d" % cc], ps_free_b[cc]]
                        stop_evs["q%d" % cc] = P.op(
                            "tensor",
                            mk_mm(PSb[cc],
                                  WQS[:, (r * 8 + dc) * 256 + cc * 128:
                                      (r * 8 + dc) * 256 + cc * 128 + 128],
                                  Dg[:, dc * 512:(dc + 1) * 512], start, stop),
                            after=aft, inc="sP" if stop else None)
                    for cc in range(2):
                        aft = [ev_xTe[dc], ev_wc[1][r * 2 + (1 if dc >= 4 else 0)]]
                        if start:
                            aft += [bfree["PS%d" % (2 + cc)], ps_free_b[2 + cc]]
                        stop_evs["k%d" % cc] = P.op(
                            "tensor",
                            mk_mm(PSb[2 + cc],
                                  WKS[:, (r * 8 + dc) * 256 + cc * 128:
                                      (r * 8 + dc) * 256 + cc * 128 + 128],
                                  Dg[:, dc * 512:(dc + 1) * 512], start, stop),
                            after=aft, inc="sP" if stop else None)
                    for kc in range(KC):
                        bank = P_A if kc < 2 else P_B
                        half = kc % 2
                        aft = [ev_xTe[dc], ev_wc[2][r * 2 + (1 if dc >= 4 else 0)]]
                        if start:
                            aft += [bfree["PA" if kc < 2 else "PB"]]
                        stop_evs["v%d" % kc] = P.op(
                            "tensor",
                            mk_mm(bank[:, half * 256: half * 256 + 256],
                                  Dg[:, dc * 512 + kc * 128: dc * 512 + kc * 128 + 128],
                                  WVS[:, (r * 8 + dc) * 256:(r * 8 + dc) * 256 + 256],
                                  start and half == 0, stop),
                            after=aft, inc="sP" if stop else None)

                for dc in range(DCN):
                    lastt = None
                    for tc in range(TC):
                        aft = [ev_xbf[tc], ev_idb, pt_free]
                        lastt = P.op("tensor",
                                     mk_tr(P_T[:, tc * 128: tc * 128 + 128],
                                           C1[:, tc * 1024 + dc * 128: tc * 1024 + dc * 128 + 128],
                                           IDB[:, :]),
                                     after=aft, inc="sP" if tc == TC - 1 else None)
                    aft = [lastt]
                    if g >= 2:
                        aft.append(st[g - 2]["ymm_all"])
                    ev_xTe[dc] = P.op("scalar",
                                      mk_act(Dg[:, dc * 512:(dc + 1) * 512], P_T[:, 0:512],
                                             AF.Identity),
                                      after=aft, inc="sA")
                    pt_free = ev_xTe[dc]
                    if dc >= 1:
                        qkv_mm_block(dc - 1)
                qkv_mm_block(DCN - 1)
                s["T1done"] = ev_xTe[DCN - 1]
                s["xTe"] = ev_xTe

                ev_qe = [None, None]
                ev_ke = [None, None]
                for cc in range(2):
                    ev_qe[cc] = P.op("scalar",
                                     mk_act(QF[:, cc * 512:(cc + 1) * 512], PSb[cc],
                                            AF.Identity,
                                            bias=BQC[:, r * 2 + cc: r * 2 + cc + 1]),
                                     after=[stop_evs["q%d" % cc], ev_bqkc], inc="sA")
                    bfree["PS%d" % cc] = ev_qe[cc]
                for cc in range(2):
                    ev_ke[cc] = P.op("scalar",
                                     mk_act(KF[:, cc * 512:(cc + 1) * 512], PSb[2 + cc],
                                            AF.Identity,
                                            bias=BKC[:, r * 2 + cc: r * 2 + cc + 1]),
                                     after=[stop_evs["k%d" % cc], ev_bqkc], inc="sA")
                    bfree["PS%d" % (2 + cc)] = ev_ke[cc]
                s["v_stop_evs"] = stop_evs
                s["qe"], s["ke"] = ev_qe, ev_ke

            def emit_v_evacs(g):
                r = g % 4
                s = st[g]
                stop_evs = s["v_stop_evs"]
                ev_ve = [None] * KC
                for kc in range(KC):
                    bank = P_A if kc < 2 else P_B
                    half = kc % 2
                    aft = [stop_evs["v%d" % kc], stop_evs["v%d" % (kc ^ 1)],
                           ev_bvb, ev_vones]
                    if g >= 1:
                        aft.append(st[g - 1]["cmm_all"])
                    ev_ve[kc] = P.op("vector",
                                     mk_tt(va4[:, kc:kc + 1, :, 0:64],
                                           bank[:, half * 256: half * 256 + 256]
                                           .rearrange("p (o h w) -> p o h w", o=1, h=4),
                                           BVB[:, r * 256:(r + 1) * 256]
                                           .rearrange("p (o h w) -> p o h w", o=1, h=4),
                                           ALU.add),
                                     after=aft, inc="sV")
                bfree["PA"] = ev_ve[1]
                bfree["PB"] = ev_ve[3]
                s["ve"] = ev_ve
                s["qkv_done"] = ev_ve[KC - 1]

            def phase_T2_FFN_front(g):
                P.phase = "T2front g%d" % g
                nonlocal pt_free
                s = st[g]
                Dg = D[g % 2]
                ev_x1Te = [None] * DCN
                xa_stop = [None]

                def xa_mm(dc):
                    start, stop = dc == 0, dc == DCN - 1
                    aft = [ev_x1Te[dc]]
                    if start:
                        aft += [ev_aas, bfree["PS0"], ps_free]
                    xa_stop[0] = P.op("tensor",
                                      mk_mm(PSb[0][0:16, :], AAS[:, dc * 16:(dc + 1) * 16],
                                            Dg[:, dc * 512:(dc + 1) * 512], start, stop),
                                      after=aft, inc="sP" if stop else None)

                for dc in range(DCN):
                    lastt = None
                    for tc in range(TC):
                        aft = [s["x1bf"][tc], pt_free]
                        lastt = P.op("tensor",
                                     mk_tr(P_T[:, tc * 128: tc * 128 + 128],
                                           C2[:, tc * 1024 + dc * 128: tc * 1024 + dc * 128 + 128],
                                           IDB[:, :]),
                                     after=aft, inc="sP" if tc == TC - 1 else None)
                    ev_x1Te[dc] = P.op("scalar",
                                       mk_act(Dg[:, dc * 512:(dc + 1) * 512], P_T[:, 0:512],
                                              AF.Identity),
                                       after=[lastt, s["qkv_done"]], inc="sA")
                    pt_free = ev_x1Te[dc]
                    if dc >= 1:
                        xa_mm(dc - 1)
                xa_mm(DCN - 1)
                s["x1Te"] = ev_x1Te
                s["T2done"] = ev_x1Te[DCN - 1]
                ev_xae = P.op("scalar",
                              mk_act(XAS[0:16, :], PSb[0][0:16, :], AF.Identity),
                              after=[xa_stop[0]], inc="sA")
                bfree["PS0"] = ev_xae
                s["xae"] = ev_xae
                s["ymm"] = [None] * 8
                s["ye"] = [None] * 8
                s["ychains_done"] = 0

            def emit_y_chains(g, n, defer=None):
                P.phase = "ych g%d" % g
                s = st[g]
                Dg = D[g % 2]
                i0 = s["ychains_done"]
                for i in range(i0, min(i0 + n, 8)):
                    if defer is not None:
                        for pair in list(defer):
                            if pair[1] == i - 2:
                                emit_ye(*pair)
                                defer.remove(pair)
                    tc, fh = i // 2, i % 2
                    bank = P_A if i % 2 == 0 else P_B
                    bk = "PA" if i % 2 == 0 else "PB"
                    for dc in range(DCN):
                        aft = [s["x1Te"][dc], ev_wf_s[dc]]
                        if dc == 0:
                            aft += [bfree[bk]]
                        P.op("tensor",
                             mk_mm(bank[:, :],
                                   Dg[:, dc * 512 + tc * 128: dc * 512 + tc * 128 + 128],
                                   WFS[:, dc * 1024 + fh * 512: dc * 1024 + fh * 512 + 512],
                                   dc == 0, False),
                             after=aft)
                    s["ymm"][i] = P.op("tensor",
                                       mk_mm(bank[:, :], XAS[:, tc * 128:(tc + 1) * 128],
                                             BAS[:, fh * 512:(fh + 1) * 512],
                                             False, True),
                                       after=[s["xae"], ev_bas, ev_xasones, ev_basbias],
                                       inc="sP")
                    if defer is None:
                        emit_ye(g, i)
                    else:
                        defer.append((g, i))
                s["ychains_done"] = min(i0 + n, 8)
                if s["ychains_done"] == 8:
                    s["ymm_all"] = s["ymm"][7]

            def emit_ye(g, i):
                s = st[g]
                Ag = A[g % 3]
                tc = i // 2
                bank = P_A if i % 2 == 0 else P_B
                bk = "PA" if i % 2 == 0 else "PB"
                col = tc * 1024 + (i % 2) * 512
                s["ye"][i] = P.op("vector",
                                  mk_stt(Ag[:, col: col + 512], bank[:, :], 1.0,
                                         B[:, col: col + 512], ALU.mult, ALU.add,
                                         accum=SC[:, Z2H + i: Z2H + i + 1]),
                                  after=[s["ymm"][i], s["sq1"], s["ap1"][tc],
                                         s["red_all"][0], s["red_all"][2],
                                         s["x1bf"][tc]],
                                  inc="sV")
                bfree[bk] = s["ye"][i]

            def phase_attention(g, gm1):
                nonlocal ps_free
                r = g % 4
                Ag = A[g % 3]
                s = st[g]
                ev_qe, ev_ke, ev_ve = s["qe"], s["ke"], s["ve"]
                ev_exp = [None] * NH
                ev_expk = [[None] * KC for _ in range(NH)]
                ev_cmm = [None] * NH
                ev_ctxe = [None] * NH
                for h in range(NH):
                    P.phase = "attn g%d h%d" % (g, h)
                    pt = PT[:, (h % 2) * 2048:(h % 2) * 2048 + 2048]
                    kbase = (h % 2) * 64
                    cchunk = h // 2
                    ev_smm = [None] * KC
                    for kc in range(KC):
                        aft = [ev_qe[cchunk], ev_ke[cchunk]]
                        if h >= 1:
                            aft.append(ev_expk[h - 1][kc])
                        else:
                            aft += [ev_qe[1], ev_ke[1], bfree["PS0"], bfree["PS1"],
                                    bfree["PS2"], bfree["PS3"],
                                    ps_free_b[0], ps_free_b[1],
                                    ps_free_b[2], ps_free_b[3]]
                        ev_smm[kc] = P.op("tensor",
                                          mk_mm(PSb[kc],
                                                KF[kbase:kbase + 64,
                                                   cchunk * 512 + kc * 128: cchunk * 512 + kc * 128 + 128],
                                                QF[kbase:kbase + 64,
                                                   cchunk * 512: cchunk * 512 + 512],
                                                True, True),
                                          after=aft, inc="sP")
                    ye_defer = []
                    n_y = 2 if g != 1 else (0, 0, 4, 4)[h]
                    if gm1 is not None and n_y:
                        emit_y_chains(gm1, n_y, defer=ye_defer)
                        P.phase = "attn g%d h%d" % (g, h)
                    e1 = None
                    for kc in range(KC):
                        dst = SC[:, (MCOL if kc == 0 else MXP + kc - 1):
                                 (MCOL if kc == 0 else MXP + kc - 1) + 1]
                        e1 = P.op("vector", mk_red(dst, PSb[kc], ALU.max),
                                  after=[ev_smm[kc]], inc="sV" if kc == KC - 1 else None)
                    ecomb = P.op("vector",
                                 mk_tt(SC[:, MCOL:MCOL + 1]
                                       .rearrange("p (a b) -> p a b", b=1),
                                       SC[:, MCOL:MCOL + 1]
                                       .rearrange("p (a b) -> p a b", b=1),
                                       SC[:, LNT:LNT + 3]
                                       .rearrange("p (a b) -> p a b", b=3)[:, :, 0:1]
                                       if False else SC[:, MXP:MXP + 1]
                                       .rearrange("p (a b) -> p a b", b=1),
                                       ALU.max),
                                 after=[e1], inc="sV")
                    ecomb = P.op("vector",
                                 mk_tt(SC[:, MCOL:MCOL + 1], SC[:, MCOL:MCOL + 1],
                                       SC[:, MXP + 1:MXP + 2], ALU.max),
                                 after=[ecomb], inc="sV")
                    ecomb = P.op("vector",
                                 mk_tt(SC[:, MCOL:MCOL + 1], SC[:, MCOL:MCOL + 1],
                                       SC[:, MXP + 2:MXP + 3], ALU.max),
                                 after=[ecomb], inc="sV")
                    aftc = [ecomb]
                    aftc.append(ev_ctxe[h - 1] if h >= 1 else
                                (st[g - 1]["ctxe_all"] if g >= 1 else None))
                    e2 = P.op("tensor",
                              mk_tr(P_C[0:1, 264:392], SC[:, MCOL:MCOL + 1], IDF[:, :]),
                              after=aftc, inc="sP")
                    e3 = P.op("vector", mk_red(SC[0:1, MG:MG + 1], P_C[0:1, 264:392],
                                               ALU.max),
                              after=[e2], inc="sV")
                    e4 = P.op("tensor",
                              mk_mm(P_C[:, 392:393], ONE1[0:1, :], SC[0:1, MG:MG + 1],
                                    True, True),
                              after=[e3], inc="sP")
                    e5 = P.op("vector",
                              mk_ts(SC[:, EB:EB + 1], P_C[:, 392:393], -0.125, None,
                                    ALU.mult),
                              after=[e4], inc="sV")
                    wb = ev_cmm[h - 2] if h >= 2 else ptbuf_free[h % 2]
                    for kc in range(KC):
                        ev_expk[h][kc] = P.op("scalar",
                                              mk_act(pt[:, kc * 512:(kc + 1) * 512],
                                                     PSb[kc], AF.Exp,
                                                     bias=SC[:, EB:EB + 1], scale=0.125),
                                              after=[e5, wb], inc="sA")
                    ev_exp[h] = ev_expk[h][KC - 1]
                    evc = None
                    for qc in range(TC):
                        for kc in range(KC):
                            aft = [ev_expk[h][kc], ev_ve[kc]]
                            evc = P.op("tensor",
                                       mk_mm(P_C[:, qc * 66: qc * 66 + 66],
                                             pt[:, kc * 512 + qc * 128: kc * 512 + qc * 128 + 128],
                                             VA[:, kc * 264 + h * 66: kc * 264 + h * 66 + 66],
                                             kc == 0, kc == KC - 1),
                                       after=aft,
                                       inc="sP" if (qc == TC - 1 and kc == KC - 1) else None)
                    ev_cmm[h] = evc
                    ptbuf_free[h % 2] = evc
                    pc66 = P_C[:, 0:264].rearrange("p (qc w) -> p qc w", w=66)
                    e6 = P.op("vector",
                              lambda e, d=SC[:, RR4:RR4 + 4].rearrange("p (a b) -> p a b", b=1),
                              s_=pc66[:, :, 64:65]: e.reciprocal(d, s_),
                              after=[evc], inc="sV")
                    eve = None
                    for qc in range(TC):
                        col = qc * 1024 + r * 64 + 256 * h
                        eve = P.op("vector",
                                   mk_stt(Ag[:, col: col + 64],
                                          P_C[:, qc * 66: qc * 66 + 64],
                                          SC[:, RR4 + qc: RR4 + qc + 1],
                                          Ag[:, col: col + 64],
                                          ALU.mult, ALU.add),
                                   after=[e6], inc="sV" if qc == TC - 1 else None)
                    ev_ctxe[h] = eve
                    for (gg, ii) in ye_defer:
                        emit_ye(gg, ii)
                    if g == 1 and h < 2:
                        emit_peel_stripe(h + 6)
                        P.phase = "attn g%d h%d" % (g, h)
                ps_free = ev_exp[NH - 1]
                for _kc in range(KC):
                    ps_free_b[_kc] = ev_expk[NH - 1][_kc]
                s["cmm_all"] = ev_cmm[NH - 1]
                s["ctxe_all"] = ev_ctxe[NH - 1]
                s["exp"] = ev_exp

            def ln_stats(sum_c, ssq_c, m_c, r_c, nmr_c, aft):
                ea = P.op("vector",
                          mk_ts(SC[:, m_c:m_c + 4], SC[:, sum_c:sum_c + 4],
                                1.0 / 1024.0, None, ALU.mult),
                          after=aft, inc="sV")
                eb_ = P.op("vector",
                           mk_ts(SC[:, VART:VART + 4], SC[:, ssq_c:ssq_c + 4],
                                 1.0 / 1024.0, None, ALU.mult),
                           after=aft, inc="sV")
                ec = P.op("vector",
                          mk_tt(SC[:, LNT:LNT + 4], SC[:, m_c:m_c + 4],
                                SC[:, m_c:m_c + 4], ALU.mult),
                          after=[ea], inc="sV")
                ed = P.op("vector",
                          mk_tt(SC[:, VART:VART + 4], SC[:, VART:VART + 4],
                                SC[:, LNT:LNT + 4], ALU.subtract),
                          after=[eb_, ec], inc="sV")
                ed = P.op("vector",
                          mk_ts(SC[:, VART:VART + 4], SC[:, VART:VART + 4],
                                0.0, None, ALU.max),
                          after=[ed], inc="sV")
                ee = P.op("scalar",
                          mk_act(SC[:, LNT:LNT + 4], SC[:, VART:VART + 4], AF.Ln,
                                 bias=SC[:, 63:64]),
                          after=[ed, ev_eps], inc="sA")
                ef = P.op("scalar",
                          mk_act(SC[:, r_c:r_c + 4], SC[:, LNT:LNT + 4], AF.Exp,
                                 scale=-0.5),
                          after=[ee], inc="sA")
                eg = P.op("vector",
                          mk_stt(SC[:, nmr_c:nmr_c + 4], SC[:, m_c:m_c + 4], -1.0,
                                 SC[:, r_c:r_c + 4], ALU.mult, ALU.mult),
                          after=[ef], inc="sV")
                return eg

            def emit_ap1(g):
                s = st[g]
                Ag = A[g % 3]
                ag_tile = [Ag[:, tc * 1024:(tc + 1) * 1024] for tc in range(TC)]
                ev_ap1 = [None] * TC
                for tc in range(TC):
                    aft = [s["st1"]]
                    if g >= 1:
                        aft += [st[g - 1]["ye"][7] if st[g - 1]["ye"][7] else None,
                                ev_store[g - 1]]
                    ev_ap1[tc] = P.op("vector",
                                      mk_ts(B[:, tc * 1024:(tc + 1) * 1024], ag_tile[tc],
                                            SC[:, R1 + tc: R1 + tc + 1],
                                            SC[:, NMR1 + tc: NMR1 + tc + 1],
                                            ALU.mult, ALU.add),
                                      after=aft, inc="sV")
                s["ap1"] = ev_ap1

            def phase_LN1(g, defer_ap1=False):
                P.phase = "LN1 g%d" % g
                r = g % 4
                Ag = A[g % 3]
                s = st[g]
                ag_tile = [Ag[:, tc * 1024:(tc + 1) * 1024] for tc in range(TC)]
                ev_redv = None
                for tc in (0, 1):
                    ev_redv = P.op("vector",
                                   mk_red(SC[:, ZS + tc: ZS + tc + 1], ag_tile[tc], ALU.add),
                                   after=[s["ctxe_all"]],
                                   inc="sV" if tc == 1 else None)
                for tc in (2, 3):
                    P.op("scalar",
                         mk_act(DSCA[:, :], ag_tile[tc], AF.Identity,
                                accum=SC[:, ZS + tc: ZS + tc + 1]),
                         after=[s["ctxe_all"]], inc=None)
                for tc in (0, 1):
                    P.op("scalar",
                         mk_act(DSCA[:, :], ag_tile[tc], AF.Square,
                                accum=SC[:, SSQ1 + tc: SSQ1 + tc + 1]),
                         after=[s["ctxe_all"]], inc=None)
                ev_sqa = P.op("scalar",
                              mk_act(SC[0:1, 67:68], SC[0:1, 67:68], AF.Identity),
                              after=[], inc="sA")
                ev_sqv = None
                for tc in (2, 3):
                    ev_sqv = P.op("vector",
                                  mk_stt(DSCV[:, :], ag_tile[tc], 1.0, ag_tile[tc],
                                         ALU.mult, ALU.mult,
                                         accum=SC[:, SSQ1 + tc: SSQ1 + tc + 1]),
                                  after=[s["ctxe_all"]],
                                  inc="sV" if tc == 3 else None)
                s["sq1"] = ev_sqa
                s["red_all"] = (ev_redv, ev_sqa, ev_sqv)
                ev_st1 = ln_stats(ZS, SSQ1, M1, R1, NMR1, [ev_redv, ev_sqa, ev_sqv])
                s["st1"] = ev_st1
                ev_x1bf = [None] * TC
                for tc in range(TC):
                    aft2 = [ev_st1]
                    if g >= 1:
                        aft2.append(st[g - 1]["T2done"])
                    ev_x1bf[tc] = P.op("scalar",
                                       mk_act(C2[:, tc * 1024:(tc + 1) * 1024], ag_tile[tc],
                                              AF.Identity,
                                              bias=SC[:, NMR1 + tc: NMR1 + tc + 1],
                                              scale=SC[:, R1 + tc: R1 + tc + 1]),
                                       after=aft2, inc="sA")
                s["x1bf"] = ev_x1bf
                if not defer_ap1:
                    emit_ap1(g)

            def phase_LN2_store(g):
                P.phase = "LN2 g%d" % g
                s = st[g]
                Ag = A[g % 3]
                ag_tile = [Ag[:, tc * 1024:(tc + 1) * 1024] for tc in range(TC)]
                z2h = SC[:, Z2H:Z2H + 8].rearrange("p (tc f) -> p tc f", f=2)
                e1 = P.op("vector",
                          mk_tt(SC[:, Z2S:Z2S + 4].rearrange("p (a b) -> p a b", b=1),
                                z2h[:, :, 0:1], z2h[:, :, 1:2], ALU.add),
                          after=[s["ye"][7]], inc="sV")
                ev_sq2 = None
                for tc in range(TC):
                    ev_sq2 = P.op("scalar",
                                  mk_act(DSCA[:, :], ag_tile[tc], AF.Square,
                                         accum=SC[:, SSQ2 + tc: SSQ2 + tc + 1]),
                                  after=[s["ye"][tc * 2 + 1]],
                                  inc="sA" if tc == TC - 1 else None)
                ev_sq2 = P.op("scalar",
                              mk_act(SC[0:1, 67:68], SC[0:1, 67:68], AF.Identity),
                              after=[ev_sq2], inc="sA")
                s["sq2"] = ev_sq2
                ev_st2 = ln_stats(Z2S, SSQ2, M1, R2, NMR2, [e1, ev_sq2])
                ev_ap2 = None
                for tc in range(TC):
                    ev_ap2 = P.op("vector",
                                  mk_ts(B[:, tc * 1024:(tc + 1) * 1024], ag_tile[tc],
                                        SC[:, R2 + tc: R2 + tc + 1],
                                        SC[:, NMR2 + tc: NMR2 + tc + 1],
                                        ALU.mult, ALU.add),
                                  after=[ev_st2, s["ye"][7]], inc="sV")
                ev_afree[g] = ev_ap2
                ev_store[g] = P.op("sync", dma(xstore_ap(g), t3(B)),
                                   after=[ev_ap2], inc="sQ", inc_amt=16)

            st[0]["xbf"] = ev_xbf0
            for g in range(NG):
                if g + 1 < NG and g >= 2:
                    ev_load[g + 1] = P.op("sync", dma(t3(A[(g + 1) % 3]), xload_ap(g + 1)),
                                          after=[ev_afree[g - 2]],
                                          inc="sQ", inc_amt=16)
                phase_T1_QKV(g)
                if g + 1 < NG and g != 1:
                    emit_xbf(g + 1)
                if g == 0:
                    emit_quant_rd2()
                if g >= 1:
                    phase_LN1(g - 1)
                if g >= 1:
                    phase_T2_FFN_front(g - 1)
                if g == 1:
                    emit_peel_stripe(4)
                emit_v_evacs(g)
                if g == 1:
                    emit_peel_stripe(5)
                phase_attention(g, g - 1 if g >= 1 else None)
                if g == 0:
                    emit_quant_scale()
                if g == 1:
                    ev_load[2] = P.op("sync", dma(t3(A[2]), xload_ap(2)),
                                      after=[qst["e3p"][4], qst["e3p"][5],
                                             qst["e3p"][6], qst["e3p"][7],
                                             qst["fin"][6], qst["fin"][7]],
                                      inc="sQ", inc_amt=16)
                    emit_xbf(2)
                if g >= 1:
                    emit_y_chains(g - 1, 8)
                    phase_LN2_store(g - 1)
                if g == 0:
                    for _s in range(4):
                        emit_peel_stripe(_s)
            phase_LN1(NG - 1)
            phase_T2_FFN_front(NG - 1)
            emit_y_chains(NG - 1, 8)
            phase_LN2_store(NG - 1)

            P.op("sync", lambda e: e.nop(), after=[ev_store[NG - 1]])
            P.emit(block)

    return nc


_LAST_PROG = None
_NC_CACHE = None


def make_in_maps(L0_attn_X, L0_attn_Wq, L0_attn_bq, L0_attn_Wkv, L0_attn_bkv,
                 L0_ffn_kernel, L0_ffn_bias, L0_ffn_a, L0_ffn_b):
    X = np.ascontiguousarray(np.asarray(L0_attn_X, dtype=np.float32)).reshape(32768, 1024)
    def _wperm(w):
        return np.ascontiguousarray(
            np.asarray(w, dtype=np.float32).reshape(8, 128, 4, 4, 64)
            .transpose(1, 3, 0, 2, 4).reshape(128, 8192))

    wkv_f = np.asarray(L0_attn_Wkv, dtype=np.float32)
    wkvb_f = np.asarray(L0_attn_bkv, dtype=np.float32)

    def _bcols(b):
        b4 = np.asarray(b, dtype=np.float32).reshape(4, 4, 64)
        cols = [b4[2 * cc:2 * cc + 2, r, :].reshape(128)
                for r in range(4) for cc in range(2)]
        return np.ascontiguousarray(np.stack(cols, axis=1))

    common = {
        "wqs": _wperm(L0_attn_Wq),
        "wks": _wperm(wkv_f[:, 0:1024]),
        "wvs": _wperm(wkv_f[:, 1024:2048]),
        "bqc": _bcols(L0_attn_bq),
        "bkc": _bcols(wkvb_f[0:1024]),
        "bvb": np.ascontiguousarray(
            np.tile(wkvb_f[1024:2048].reshape(4, 4, 64).transpose(1, 0, 2)
                    .reshape(1, 1024), (128, 1))),
        "fk": np.ascontiguousarray(L0_ffn_kernel, dtype=np.float32),
        "fbias": np.ascontiguousarray(L0_ffn_bias, dtype=np.float32),
        "fa": np.ascontiguousarray(L0_ffn_a, dtype=np.float32),
        "fb": np.ascontiguousarray(L0_ffn_b, dtype=np.float32),
        "identb": np.eye(128, dtype=ml_dtypes.bfloat16),
        "identf": np.eye(128, dtype=np.float32),
    }
    in_maps = []
    for c in range(NCORE):
        m = dict(common)
        m["xin"] = np.ascontiguousarray(X[c * 4096:(c + 1) * 4096, :])
        in_maps.append(m)
    return in_maps


def kernel(**inputs):
    global _NC_CACHE
    in_maps = make_in_maps(**inputs)
    if _NC_CACHE is None:
        _NC_CACHE = build_nc()
    res = run_bass_kernel_spmd(_NC_CACHE, in_maps, core_ids=list(range(NCORE)))
    out = np.concatenate([r["out"] for r in res.results], axis=0)
    return out.reshape(4, 8192, 1024)


# version 46 (final)
# speedup vs baseline: 1.0050x; 1.0050x over previous
import numpy as np
import ml_dtypes
import concourse.bass as bass
import concourse.mybir as mybir
from concourse.bass_utils import run_bass_kernel_spmd

F32 = mybir.dt.float32
BF16 = mybir.dt.bfloat16
AF = mybir.ActivationFunctionType
ALU = mybir.AluOpType
AX = mybir.AxisListType

NCORE = 8
NG = 8
TC = 4
DCN = 8
NH = 4
KC = 4
EPS = 1e-5

ENGS = ("sync", "tensor", "vector", "scalar", "gpsimd")


class Prog:

    def __init__(self, sems):
        self.sems = sems
        self.cnt = {k: 0 for k in sems}
        self.ops = {e: [] for e in ENGS}
        self.tags = {}
        self.phase = "pre"
        self.last_wait = {e: {} for e in ENGS}

    def op(self, eng, fn, after=(), inc=None, inc_amt=1, tag=None):
        self.tags.setdefault(eng, []).append(tag or self.phase)
        waits = []
        for ev in after:
            if ev is None:
                continue
            s, v = ev
            lw = self.last_wait[eng].get(s, -1)
            if v > lw:
                self.last_wait[eng][s] = v
                waits.append((self.sems[s], v))
        ev_out = None
        handle = None
        if inc is not None:
            self.cnt[inc] += inc_amt
            ev_out = (inc, self.cnt[inc])
            handle = self.sems[inc]
        self.ops[eng].append((fn, waits, handle, inc_amt))
        return ev_out

    def emit(self, block):
        def make(eng_name):
            oplist = self.ops[eng_name]

            def body(e):
                for (fn, waits, inc_h, inc_amt) in oplist:
                    for (h, v) in waits:
                        e.wait_ge(h, v)
                    inst = fn(e)
                    if inc_h is not None:
                        inst.then_inc(inc_h, inc_amt)
            return body

        block.sync(make("sync"))
        block.tensor(make("tensor"))
        block.vector(make("vector"))
        block.scalar(make("scalar"))
        block.gpsimd(make("gpsimd"))


def build_nc():
    nc = bass.Bass("TRN2", target_bir_lowering=False)

    xin = nc.dram_tensor("xin", [4096, 1024], F32, kind="ExternalInput")
    wqs_h = nc.dram_tensor("wqs", [128, 8192], F32, kind="ExternalInput")
    wks_h = nc.dram_tensor("wks", [128, 8192], F32, kind="ExternalInput")
    wvs_h = nc.dram_tensor("wvs", [128, 8192], F32, kind="ExternalInput")
    bqc_h = nc.dram_tensor("bqc", [128, 8], F32, kind="ExternalInput")
    bkc_h = nc.dram_tensor("bkc", [128, 8], F32, kind="ExternalInput")
    bvb_h = nc.dram_tensor("bvb", [128, 1024], F32, kind="ExternalInput")
    fk_h = nc.dram_tensor("fk", [1024, 1024], F32, kind="ExternalInput")
    fbias_h = nc.dram_tensor("fbias", [1024], F32, kind="ExternalInput")
    fa_h = nc.dram_tensor("fa", [1024, 16], F32, kind="ExternalInput")
    fb_h = nc.dram_tensor("fb", [16, 1024], F32, kind="ExternalInput")
    idb_h = nc.dram_tensor("identb", [128, 128], BF16, kind="ExternalInput")
    idf_h = nc.dram_tensor("identf", [128, 128], F32, kind="ExternalInput")
    out_h = nc.dram_tensor("out", [4096, 1024], F32, kind="ExternalOutput")

    from contextlib import ExitStack
    es = ExitStack()
    sb = lambda name, shape, dt: es.enter_context(nc.sbuf_tensor(name, shape, dt))
    ps = lambda name, shape, dt: es.enter_context(nc.psum_tensor(name, shape, dt))

    with es:
        A = [sb("A0", [128, 4096], F32), sb("A1", [128, 4096], F32),
             sb("A2", [128, 4096], F32)]
        B = sb("B", [128, 4096], F32)
        C1b = [sb("C1", [128, 4096], BF16), sb("C1x", [128, 4096], BF16)]
        C2 = sb("C2", [128, 4096], BF16)
        D = [sb("D0", [128, 4096], BF16), sb("D1", [128, 4096], BF16)]
        QF = sb("QF", [128, 1024], BF16)
        KF = sb("KF", [128, 1024], BF16)
        VA = sb("VA", [128, 1056], BF16)
        PT = sb("PT", [128, 4096], BF16)
        WQS = sb("WQS", [128, 8192], BF16)
        WKS = sb("WKS", [128, 8192], BF16)
        WVS = sb("WVS", [128, 8192], BF16)
        WFS = sb("WFS", [128, 8192], BF16)
        AAS = sb("AAS", [128, 128], BF16)
        BAS = sb("BAS", [32, 1024], BF16)
        XAS = sb("XAS", [32, 512], BF16)
        ROWB = sb("ROWB", [1, 1024], BF16)
        BQC = sb("BQC", [128, 8], F32)
        BKC = sb("BKC", [128, 8], F32)
        BVB = sb("BVB", [128, 1024], F32)
        STG = [sb("STG0", [128, 1024], F32), sb("STG1", [128, 1024], F32)]
        ONE1 = sb("ONE1", [1, 128], F32)
        ROWF = sb("ROWF", [1, 256], F32)
        SC = sb("SC", [128, 72], F32)
        DSCV = sb("DSCV", [128, 1024], BF16)
        DSCA = sb("DSCA", [128, 1024], BF16)
        IDB = sb("IDB", [128, 128], BF16)
        IDF = sb("IDF", [128, 128], F32)

        P_T = ps("P_T", [128, 1024], BF16)
        P_S = ps("P_S", [128, 2048], F32)
        P_A = ps("P_A", [128, 512], F32)
        P_B = ps("P_B", [128, 512], F32)
        P_C = ps("P_C", [128, 512], F32)

        with (
            nc.semaphore("sQ") as hQ,
            nc.semaphore("sP") as hP,
            nc.semaphore("sV") as hV,
            nc.semaphore("sA") as hA,
            nc.semaphore("sL") as hL,
            nc.Block() as block,
        ):
            P = Prog({"sQ": hQ, "sP": hP, "sV": hV, "sA": hA, "sL": hL})
            global _LAST_PROG
            _LAST_PROG = P

            ZS, SSQ1, M1, R1, NMR1 = 0, 4, 8, 12, 16
            Z2H, Z2S, SSQ2, R2, NMR2 = 20, 28, 32, 36, 40
            MCOL, EB, RR4 = 44, 45, 46
            QT2, MG, VART, LNT = 50, 51, 52, 56
            QAMX, QRS, QSC = 60, 61, 62
            MXP = 64

            def dma(dst, src):
                return lambda e, d=dst, s=src: e.dma_start(d, s)

            def mk_copy(dst, src):
                return lambda e, d=dst, s=src: e.tensor_copy(d, s)

            def mk_mm(dst, lhs, rhs, start, stop):
                return lambda e, d=dst, l=lhs, r_=rhs, st=start, sp=stop: e.matmul(
                    d, l, r_, start=st, stop=sp)

            def mk_tr(dst, src, ident):
                return lambda e, d=dst, s=src, i=ident: e.transpose(d, s, i)

            def mk_ts(dst, src, s1, s2, op0, op1=ALU.bypass, accum=None):
                if s2 is None:
                    return lambda e, d=dst, s=src, a=s1, o=op0: e.tensor_scalar(
                        d, s, a, None, o)
                return lambda e, d=dst, s=src, a=s1, b=s2, o0=op0, o1=op1, ac=accum: \
                    e.tensor_scalar(d, s, a, b, o0, o1, accum_out=ac)

            def mk_stt(dst, in0, scal, in1, op0, op1, accum=None):
                return lambda e, d=dst, i0=in0, sc=scal, i1=in1, o0=op0, o1=op1, ac=accum: \
                    e.scalar_tensor_tensor(d, i0, sc, i1, o0, o1, accum_out=ac)

            def mk_tt(dst, in0, in1, op):
                return lambda e, d=dst, i0=in0, i1=in1, o=op: e.tensor_tensor(d, i0, i1, o)

            def mk_red(dst, src, op, absval=False):
                return lambda e, d=dst, s=src, o=op, av=absval: e.tensor_reduce(
                    d, s, AX.X, o, apply_absolute_value=av)

            def mk_act(dst, src, func, bias=0.0, scale=1.0, accum=None):
                return lambda e, d=dst, s=src, f=func, b=bias, sc=scale, ac=accum: \
                    e.activation(d, s, f, bias=b, scale=sc, accum_out=ac)

            ev_idb = P.op("sync", dma(IDB[:, :], idb_h[:, :]), inc="sQ", inc_amt=16)
            ev_idf = P.op("sync", dma(IDF[:, :], idf_h[:, :]), inc="sQ", inc_amt=16)

            def xload_ap(g):
                p, r = g // 4, g % 4
                base = p * 2048 + r
                return xin[base: base + 2045: 4, :].rearrange("(tc p) c -> p tc c", p=128)

            def xstore_ap(g):
                p, r = g // 4, g % 4
                base = p * 2048 + r
                return out_h[base: base + 2045: 4, :].rearrange("(tc p) c -> p tc c", p=128)

            def t3(t):
                return t[:, :].rearrange("p (tc c) -> p tc c", c=1024)

            ev_load = [None] * NG
            ev_load[0] = P.op("sync", dma(t3(A[0]), xload_ap(0)), inc="sQ", inc_amt=16)

            ev_one1 = P.op("vector", lambda e: e.memset(ONE1[:, :], 1.0), inc="sV")
            ev_eps = P.op("vector", lambda e: e.memset(SC[:, 63:64], EPS), inc="sV")
            ev_va0 = P.op("vector", lambda e: e.memset(VA[:, :], 0.0), inc="sV")
            va4 = VA[:, :].rearrange("p (kc h w) -> p kc h w", kc=4, w=66)
            ev_vones = P.op("vector", lambda e: e.memset(va4[:, :, :, 64:65], 1.0),
                            after=[ev_va0], inc="sV")
            ev_xas0 = P.op("gpsimd", lambda e: e.memset(XAS[:, :], 0.0), inc="sL")
            ev_bas0 = P.op("gpsimd", lambda e: e.memset(BAS[:, :], 0.0), inc="sL")
            ev_rb1 = P.op("gpsimd", lambda e: e.memset(ROWB[:, 0:512], 1.0), inc="sL")
            ev_xasones = P.op("sync", dma(XAS[16:17, :], ROWB[0:1, 0:512]),
                              after=[ev_xas0, ev_rb1], inc="sQ", inc_amt=16)

            P.op("sync", dma(BQC[:, :], bqc_h[:, :]), inc="sQ", inc_amt=16)
            ev_bqkc = P.op("sync", dma(BKC[:, :], bkc_h[:, :]), inc="sQ", inc_amt=16)
            ev_bvb = P.op("sync", dma(BVB[:, :], bvb_h[:, :]), inc="sQ", inc_amt=16)

            ev_fbrow = P.op("sync", dma(STG[0][0:1, :],
                                        fbias_h[:].rearrange("(o c) -> o c", o=1)),
                            inc="sQ", inc_amt=16)
            ev_rb2 = P.op("gpsimd", mk_copy(ROWB[0:1, :], STG[0][0:1, :]),
                          after=[ev_fbrow, ev_xasones], inc="sL")
            ev_basbias = P.op("sync", dma(BAS[16:17, :], ROWB[0:1, :]),
                              after=[ev_rb2, ev_bas0], inc="sQ", inc_amt=16)

            ev_fa = P.op("sync", dma(STG[1][:, 0:128].rearrange("p (dc c) -> p dc c", c=16),
                                     fa_h[:, :].rearrange("(dc p) c -> p dc c", p=128)),
                         after=[ev_bvb], inc="sQ", inc_amt=16)
            ev_aas = P.op("gpsimd", mk_copy(AAS[:, :], STG[1][:, 0:128]),
                          after=[ev_fa], inc="sL")

            ev_fb = P.op("sync", dma(STG[0][0:16, :], fb_h[:, :]),
                         after=[ev_rb2], inc="sQ", inc_amt=16)
            ev_bas = P.op("gpsimd", mk_copy(BAS[0:16, :], STG[0][0:16, :]),
                          after=[ev_fb, ev_bas0], inc="sL")

            ev_xbf0 = []
            for _tc in range(TC):
                ev_xbf0.append(P.op("gpsimd",
                                    mk_copy(C1b[0][:, _tc * 1024:(_tc + 1) * 1024],
                                            A[0][:, _tc * 1024:(_tc + 1) * 1024]),
                                    after=[ev_load[0]], inc="sL"))
            stg_free = [ev_bas, ev_aas]
            ev_wc = {0: [None] * 8, 1: [None] * 8, 2: [None] * 8}
            pi = 0
            for r in range(4):
                for wi in range(3):
                    src_h = (wqs_h, wks_h, wvs_h)[wi]
                    dst = (WQS, WKS, WVS)[wi]
                    for half in range(2):
                        ck = r * 2 + half
                        par = pi % 2
                        pi += 1
                        evd = P.op("sync",
                                   dma(STG[par][:, :], src_h[:, ck * 1024:(ck + 1) * 1024]),
                                   after=[stg_free[par]], inc="sQ", inc_amt=16)
                        ev_wc[wi][ck] = P.op("vector",
                                             mk_copy(dst[:, ck * 1024:(ck + 1) * 1024],
                                                     STG[par][:, :]),
                                             after=[evd], inc="sV")
                        stg_free[par] = ev_wc[wi][ck]
                if r == 0:
                    ev_load[1] = P.op("sync", dma(t3(A[1]), xload_ap(1)),
                                      inc="sQ", inc_amt=16)

            QHALF, NEG8S = 69, 68
            ev_q85 = P.op("vector", lambda e: e.memset(SC[:, QHALF:QHALF + 1], 8.5),
                          inc="sV")
            a2q = [A[2][:, i * 1024:(i + 1) * 1024] for i in range(4)]
            ev_qdma = [None] * 8
            ev_qred = [None] * 8
            for qi in range(4):
                dc = qi + 4
                ev_qdma[dc] = P.op("gpsimd", dma(a2q[qi], fk_h[dc * 128:(dc + 1) * 128, :]),
                                   after=[ev_xbf0[TC - 1]] if qi == 0 else [],
                                   inc="sL", inc_amt=16)
            for qi in range(4):
                dc = qi + 4
                ev_qred[dc] = P.op("vector",
                                   mk_red(SC[:, QT2:QT2 + 1], a2q[qi], ALU.max,
                                          absval=True),
                                   after=[ev_qdma[dc]], inc="sV")
                if dc == 4:
                    ev_qred[dc] = P.op("vector",
                                       mk_copy(SC[:, QAMX:QAMX + 1], SC[:, QT2:QT2 + 1]),
                                       after=[ev_qred[dc]], inc="sV")
                else:
                    ev_qred[dc] = P.op("vector",
                                       mk_tt(SC[:, QAMX:QAMX + 1], SC[:, QAMX:QAMX + 1],
                                             SC[:, QT2:QT2 + 1], ALU.max),
                                       after=[ev_qred[dc]], inc="sV")
            qsc_state = {"ev_qscale": None}

            def emit_quant_rd2():
                for qi in range(4):
                    ev_qdma[qi] = P.op("gpsimd",
                                       dma(a2q[qi], fk_h[qi * 128:(qi + 1) * 128, :]),
                                       after=[ev_qred[qi + 4]], inc="sL", inc_amt=16)

            def emit_quant_scale():
                ph_save = P.phase
                P.phase = "quant"
                ev_amx = None
                for qi in range(4):
                    e_ = P.op("vector",
                              mk_red(SC[:, QT2:QT2 + 1], a2q[qi], ALU.max, absval=True),
                              after=[ev_qdma[qi]], inc="sV")
                    ev_amx = P.op("vector",
                                  mk_tt(SC[:, QAMX:QAMX + 1], SC[:, QAMX:QAMX + 1],
                                        SC[:, QT2:QT2 + 1], ALU.max),
                                  after=[e_, ev_amx], inc="sV")
                ev_td1 = P.op("sync", dma(ROWF[0:1, 0:128], SC[:, QAMX:QAMX + 1]),
                              after=[ev_amx, ev_bvb, ev_bqkc], inc="sQ", inc_amt=16)
                ev_rr = P.op("vector", mk_red(SC[0:1, QT2:QT2 + 1], ROWF[0:1, 0:128],
                                              ALU.max),
                             after=[ev_td1], inc="sV")
                ev_bc = P.op("vector",
                             mk_ts(ROWF[0:1, 128:256], ONE1[0:1, :],
                                   SC[0:1, QT2:QT2 + 1], None, ALU.mult),
                             after=[ev_rr, ev_one1], inc="sV")
                ev_td2 = P.op("sync", dma(SC[:, QAMX:QAMX + 1], ROWF[0:1, 128:256]),
                              after=[ev_bc], inc="sQ", inc_amt=16)
                e_ = P.op("vector", lambda e: e.reciprocal(SC[:, QRS:QRS + 1],
                                                           SC[:, QAMX:QAMX + 1]),
                          after=[ev_td2], inc="sV")
                e_ = P.op("vector", mk_ts(SC[:, QRS:QRS + 1], SC[:, QRS:QRS + 1], 7.0,
                                          None, ALU.mult),
                          after=[e_], inc="sV")
                e_ = P.op("vector", mk_ts(SC[:, QSC:QSC + 1], SC[:, QAMX:QAMX + 1],
                                          1.0 / 7.0, None, ALU.mult),
                          after=[e_], inc="sV")
                qsc_state["ev_qscale"] = P.op("vector",
                                              mk_ts(SC[:, NEG8S:NEG8S + 1],
                                                    SC[:, QSC:QSC + 1], -8.0, None,
                                                    ALU.mult),
                                              after=[e_], inc="sV")
                P.phase = ph_save

            qst = {"e3p": [None] * 8, "fin": [None] * 8, "prevV": None}
            ev_wf_s = [None] * 8
            PEEL_Q = {0: 0, 1: 1, 2: 2, 3: 3, 4: 0, 5: 1, 6: 2, 7: 3}
            PEEL_POOL = ()

            def emit_peel_stripe(dc):
                ph_save = P.phase
                P.phase = "quant"
                qi = PEEL_Q[dc]
                on_pool = dc in PEEL_POOL
                wsl = WFS[:, dc * 1024:(dc + 1) * 1024]
                aft_eu = [qsc_state["ev_qscale"], ev_qdma[dc]]
                if dc >= 4:
                    ev_qdma[dc] = P.op("gpsimd",
                                       dma(a2q[qi], fk_h[dc * 128:(dc + 1) * 128, :]),
                                       after=[qst["e3p"][dc - 4]], inc="sL", inc_amt=16)
                    aft_eu.append(ev_qdma[dc])
                if on_pool:
                    off = 0 if qi == 2 else 2048
                    B0 = B[:, off:off + 1024]
                    B1 = B[:, off + 1024:off + 2048]
                    if dc >= 4:
                        aft_eu += [qst["e3p"][dc - 4], qst["fin"][dc - 4]]
                else:
                    B0 = STG[0][:, :]
                    B1 = STG[1][:, :]
                    aft_eu += [stg_free[0], stg_free[1]]
                    if qst["prevV"] is not None:
                        pv = qst["prevV"]
                        aft_eu += [qst["e3p"][pv], qst["fin"][pv]]
                    qst["prevV"] = dc
                eu = P.op("vector",
                          mk_ts(B0, a2q[qi], SC[:, QRS:QRS + 1], 8.5,
                                ALU.mult, ALU.add),
                          after=aft_eu, inc="sV")
                seq = [B0, B1, a2q[qi], B1, a2q[qi]]
                eng = "gpsimd" if on_pool else "vector"
                sem = "sL" if on_pool else "sV"
                e2p = eu
                for si, k in enumerate((8.0, 4.0, 2.0, 1.0)):
                    if on_pool:
                        ebp = P.op(eng, mk_ts(wsl, seq[si], k, k,
                                              ALU.is_ge, ALU.mult),
                                   after=[e2p], inc=sem)
                        e2p = P.op(eng, mk_tt(seq[si + 1], seq[si], wsl,
                                              ALU.subtract),
                                   after=[ebp], inc=sem)
                    else:
                        ebp = P.op(eng, mk_ts(wsl, seq[si], k, None, ALU.is_ge),
                                   after=[e2p], inc=sem)
                        e2p = P.op(eng,
                                   mk_stt(seq[si + 1], wsl, -k, seq[si],
                                          ALU.mult, ALU.add),
                                   after=[ebp], inc=sem)
                if on_pool:
                    qst["e3p"][dc] = P.op(eng, mk_tt(B1, B0, seq[4], ALU.subtract),
                                          after=[e2p], inc=sem)
                else:
                    qst["e3p"][dc] = P.op(eng,
                                          mk_stt(B1, seq[4], -1.0, B0,
                                                 ALU.mult, ALU.add),
                                          after=[e2p], inc=sem)
                qst["fin"][dc] = P.op("vector",
                                      mk_ts(wsl, B1, SC[:, QSC:QSC + 1],
                                            SC[:, NEG8S:NEG8S + 1],
                                            ALU.mult, ALU.add),
                                      after=[qst["e3p"][dc]], inc="sV")
                ev_wf_s[dc] = qst["fin"][dc]
                P.phase = ph_save

            PSb = [P_S[:, i * 512:(i + 1) * 512] for i in range(4)]
            ev_afree = [None] * NG
            ev_store = [None] * NG
            bfree = {"PS0": None, "PS1": None, "PS2": None, "PS3": None,
                     "PA": None, "PB": None}
            pt_free = None
            ptbuf_free = [None, None]
            ps_free = None
            st = [dict() for _ in range(NG)]

            def emit_xbf(g):
                Ag = A[g % 3]
                evs = []
                for tc in range(TC):
                    aft = [ev_load[g]]
                    if g >= 2:
                        aft.append(st[g - 2]["T1done"])
                    evs.append(P.op("gpsimd",
                                    mk_copy(C1b[g % 2][:, tc * 1024:(tc + 1) * 1024],
                                            Ag[:, tc * 1024:(tc + 1) * 1024]),
                                    after=aft, inc="sL"))
                st[g]["xbf"] = evs

            def phase_T1_QKV(g):
                P.phase = "T1QKV g%d" % g
                r = g % 4
                Ag = A[g % 3]
                nonlocal pt_free
                s = st[g]
                ev_xbf = s["xbf"]
                Dg = D[g % 2]
                ev_xTe = [None] * DCN
                stop_evs = {}

                def qkv_mm_block(dc):
                    start, stop = dc == 0, dc == DCN - 1
                    for cc in range(2):
                        aft = [ev_xTe[dc], ev_wc[0][r * 2 + (1 if dc >= 4 else 0)]]
                        if start:
                            aft += [bfree["PS%d" % cc], ps_free]
                        stop_evs["q%d" % cc] = P.op(
                            "tensor",
                            mk_mm(PSb[cc],
                                  WQS[:, (r * 8 + dc) * 256 + cc * 128:
                                      (r * 8 + dc) * 256 + cc * 128 + 128],
                                  Dg[:, dc * 512:(dc + 1) * 512], start, stop),
                            after=aft, inc="sP" if stop else None)
                    for cc in range(2):
                        aft = [ev_xTe[dc], ev_wc[1][r * 2 + (1 if dc >= 4 else 0)]]
                        if start:
                            aft += [bfree["PS%d" % (2 + cc)], ps_free]
                        stop_evs["k%d" % cc] = P.op(
                            "tensor",
                            mk_mm(PSb[2 + cc],
                                  WKS[:, (r * 8 + dc) * 256 + cc * 128:
                                      (r * 8 + dc) * 256 + cc * 128 + 128],
                                  Dg[:, dc * 512:(dc + 1) * 512], start, stop),
                            after=aft, inc="sP" if stop else None)
                    for kc in range(KC):
                        bank = P_A if kc < 2 else P_B
                        half = kc % 2
                        aft = [ev_xTe[dc], ev_wc[2][r * 2 + (1 if dc >= 4 else 0)]]
                        if start:
                            aft += [bfree["PA" if kc < 2 else "PB"]]
                        stop_evs["v%d" % kc] = P.op(
                            "tensor",
                            mk_mm(bank[:, half * 256: half * 256 + 256],
                                  Dg[:, dc * 512 + kc * 128: dc * 512 + kc * 128 + 128],
                                  WVS[:, (r * 8 + dc) * 256:(r * 8 + dc) * 256 + 256],
                                  start and half == 0, stop),
                            after=aft, inc="sP" if stop else None)

                for dc in range(DCN):
                    lastt = None
                    for tc in range(TC):
                        aft = [ev_xbf[tc], ev_idb, pt_free]
                        lastt = P.op("tensor",
                                     mk_tr(P_T[:, tc * 128: tc * 128 + 128],
                                           C1b[g % 2][:, tc * 1024 + dc * 128: tc * 1024 + dc * 128 + 128],
                                           IDB[:, :]),
                                     after=aft, inc="sP" if tc == TC - 1 else None)
                    aft = [lastt]
                    if g >= 2:
                        aft.append(st[g - 2]["ymm_all"])
                    ev_xTe[dc] = P.op("scalar",
                                      mk_act(Dg[:, dc * 512:(dc + 1) * 512], P_T[:, 0:512],
                                             AF.Identity),
                                      after=aft, inc="sA")
                    pt_free = ev_xTe[dc]
                    if dc >= 1:
                        qkv_mm_block(dc - 1)
                qkv_mm_block(DCN - 1)
                s["T1done"] = ev_xTe[DCN - 1]
                s["xTe"] = ev_xTe

                ev_qe = [None, None]
                ev_ke = [None, None]
                for cc in range(2):
                    ev_qe[cc] = P.op("scalar",
                                     mk_act(QF[:, cc * 512:(cc + 1) * 512], PSb[cc],
                                            AF.Identity,
                                            bias=BQC[:, r * 2 + cc: r * 2 + cc + 1]),
                                     after=[stop_evs["q%d" % cc], ev_bqkc], inc="sA")
                    bfree["PS%d" % cc] = ev_qe[cc]
                for cc in range(2):
                    ev_ke[cc] = P.op("scalar",
                                     mk_act(KF[:, cc * 512:(cc + 1) * 512], PSb[2 + cc],
                                            AF.Identity,
                                            bias=BKC[:, r * 2 + cc: r * 2 + cc + 1]),
                                     after=[stop_evs["k%d" % cc], ev_bqkc], inc="sA")
                    bfree["PS%d" % (2 + cc)] = ev_ke[cc]
                s["v_stop_evs"] = stop_evs
                s["qe"], s["ke"] = ev_qe, ev_ke

            def emit_v_evacs(g):
                r = g % 4
                s = st[g]
                stop_evs = s["v_stop_evs"]
                ev_ve = [None] * KC
                for kc in range(KC):
                    bank = P_A if kc < 2 else P_B
                    half = kc % 2
                    aft = [stop_evs["v%d" % kc], stop_evs["v%d" % (kc ^ 1)],
                           ev_bvb, ev_vones]
                    if g >= 1:
                        aft.append(st[g - 1]["cmm_all"])
                    ev_ve[kc] = P.op("vector",
                                     mk_tt(va4[:, kc:kc + 1, :, 0:64],
                                           bank[:, half * 256: half * 256 + 256]
                                           .rearrange("p (o h w) -> p o h w", o=1, h=4),
                                           BVB[:, r * 256:(r + 1) * 256]
                                           .rearrange("p (o h w) -> p o h w", o=1, h=4),
                                           ALU.add),
                                     after=aft, inc="sV")
                bfree["PA"] = ev_ve[1]
                bfree["PB"] = ev_ve[3]
                s["ve"] = ev_ve
                s["qkv_done"] = ev_ve[KC - 1]

            def phase_T2_FFN_front(g):
                P.phase = "T2front g%d" % g
                nonlocal pt_free
                s = st[g]
                Dg = D[g % 2]
                ev_x1Te = [None] * DCN
                xa_stop = [None]

                def xa_mm(dc):
                    start, stop = dc == 0, dc == DCN - 1
                    aft = [ev_x1Te[dc]]
                    if start:
                        aft += [ev_aas, bfree["PS0"], ps_free]
                    xa_stop[0] = P.op("tensor",
                                      mk_mm(PSb[0][0:16, :], AAS[:, dc * 16:(dc + 1) * 16],
                                            Dg[:, dc * 512:(dc + 1) * 512], start, stop),
                                      after=aft, inc="sP" if stop else None)

                for dc in range(DCN):
                    lastt = None
                    for tc in range(TC):
                        aft = [s["x1bf"][tc], pt_free]
                        lastt = P.op("tensor",
                                     mk_tr(P_T[:, tc * 128: tc * 128 + 128],
                                           C2[:, tc * 1024 + dc * 128: tc * 1024 + dc * 128 + 128],
                                           IDB[:, :]),
                                     after=aft, inc="sP" if tc == TC - 1 else None)
                    ev_x1Te[dc] = P.op("scalar",
                                       mk_act(Dg[:, dc * 512:(dc + 1) * 512], P_T[:, 0:512],
                                              AF.Identity),
                                       after=[lastt, s["qkv_done"]], inc="sA")
                    pt_free = ev_x1Te[dc]
                    if dc >= 1:
                        xa_mm(dc - 1)
                xa_mm(DCN - 1)
                s["x1Te"] = ev_x1Te
                s["T2done"] = ev_x1Te[DCN - 1]
                ev_xae = P.op("scalar",
                              mk_act(XAS[0:16, :], PSb[0][0:16, :], AF.Identity),
                              after=[xa_stop[0]], inc="sA")
                bfree["PS0"] = ev_xae
                s["xae"] = ev_xae
                s["ymm"] = [None] * 8
                s["ye"] = [None] * 8
                s["ychains_done"] = 0

            def emit_y_chains(g, n, defer=None):
                P.phase = "ych g%d" % g
                s = st[g]
                Dg = D[g % 2]
                i0 = s["ychains_done"]
                for i in range(i0, min(i0 + n, 8)):
                    if defer is not None:
                        for pair in list(defer):
                            if pair[1] == i - 2:
                                emit_ye(*pair)
                                defer.remove(pair)
                    tc, fh = i // 2, i % 2
                    bank = P_A if i % 2 == 0 else P_B
                    bk = "PA" if i % 2 == 0 else "PB"
                    for dc in range(DCN):
                        aft = [s["x1Te"][dc], ev_wf_s[dc]]
                        if dc == 0:
                            aft += [bfree[bk]]
                        P.op("tensor",
                             mk_mm(bank[:, :],
                                   Dg[:, dc * 512 + tc * 128: dc * 512 + tc * 128 + 128],
                                   WFS[:, dc * 1024 + fh * 512: dc * 1024 + fh * 512 + 512],
                                   dc == 0, False),
                             after=aft)
                    s["ymm"][i] = P.op("tensor",
                                       mk_mm(bank[:, :], XAS[:, tc * 128:(tc + 1) * 128],
                                             BAS[:, fh * 512:(fh + 1) * 512],
                                             False, True),
                                       after=[s["xae"], ev_bas, ev_xasones, ev_basbias],
                                       inc="sP")
                    if defer is None:
                        emit_ye(g, i)
                    else:
                        defer.append((g, i))
                s["ychains_done"] = min(i0 + n, 8)
                if s["ychains_done"] == 8:
                    s["ymm_all"] = s["ymm"][7]

            def emit_ye(g, i):
                s = st[g]
                Ag = A[g % 3]
                tc = i // 2
                bank = P_A if i % 2 == 0 else P_B
                bk = "PA" if i % 2 == 0 else "PB"
                col = tc * 1024 + (i % 2) * 512
                s["ye"][i] = P.op("vector",
                                  mk_stt(Ag[:, col: col + 512], bank[:, :], 1.0,
                                         B[:, col: col + 512], ALU.mult, ALU.add,
                                         accum=SC[:, Z2H + i: Z2H + i + 1]),
                                  after=[s["ymm"][i], s["sq1"], s["ap1"][tc],
                                         s["red_all"][0], s["red_all"][2],
                                         s["x1bf"][tc]],
                                  inc="sV")
                bfree[bk] = s["ye"][i]

            def phase_attention(g, gm1):
                nonlocal ps_free
                r = g % 4
                Ag = A[g % 3]
                s = st[g]
                ev_qe, ev_ke, ev_ve = s["qe"], s["ke"], s["ve"]
                ev_exp = [None] * NH
                ev_cmm = [None] * NH
                ev_ctxe = [None] * NH
                for h in range(NH):
                    P.phase = "attn g%d h%d" % (g, h)
                    pt = PT[:, (h % 2) * 2048:(h % 2) * 2048 + 2048]
                    kbase = (h % 2) * 64
                    cchunk = h // 2
                    ev_smm = [None] * KC
                    for kc in range(KC):
                        aft = [ev_qe[cchunk], ev_ke[cchunk]]
                        if h >= 1:
                            aft.append(ev_exp[h - 1])
                        else:
                            aft += [ev_qe[1], ev_ke[1], bfree["PS0"], bfree["PS1"],
                                    bfree["PS2"], bfree["PS3"]]
                        ev_smm[kc] = P.op("tensor",
                                          mk_mm(PSb[kc],
                                                KF[kbase:kbase + 64,
                                                   cchunk * 512 + kc * 128: cchunk * 512 + kc * 128 + 128],
                                                QF[kbase:kbase + 64,
                                                   cchunk * 512: cchunk * 512 + 512],
                                                True, True),
                                          after=aft, inc="sP")
                    ye_defer = []
                    n_y = 2 if g != 1 else (0, 0, 4, 4)[h]
                    if gm1 is not None and n_y:
                        emit_y_chains(gm1, n_y, defer=ye_defer)
                        P.phase = "attn g%d h%d" % (g, h)
                    e1 = None
                    for kc in range(KC):
                        dst = SC[:, (MCOL if kc == 0 else MXP + kc - 1):
                                 (MCOL if kc == 0 else MXP + kc - 1) + 1]
                        e1 = P.op("vector", mk_red(dst, PSb[kc], ALU.max),
                                  after=[ev_smm[kc]], inc="sV" if kc == KC - 1 else None)
                    ecomb = P.op("vector",
                                 mk_tt(SC[:, MCOL:MCOL + 1]
                                       .rearrange("p (a b) -> p a b", b=1),
                                       SC[:, MCOL:MCOL + 1]
                                       .rearrange("p (a b) -> p a b", b=1),
                                       SC[:, LNT:LNT + 3]
                                       .rearrange("p (a b) -> p a b", b=3)[:, :, 0:1]
                                       if False else SC[:, MXP:MXP + 1]
                                       .rearrange("p (a b) -> p a b", b=1),
                                       ALU.max),
                                 after=[e1], inc="sV")
                    ecomb = P.op("vector",
                                 mk_tt(SC[:, MCOL:MCOL + 1], SC[:, MCOL:MCOL + 1],
                                       SC[:, MXP + 1:MXP + 2], ALU.max),
                                 after=[ecomb], inc="sV")
                    ecomb = P.op("vector",
                                 mk_tt(SC[:, MCOL:MCOL + 1], SC[:, MCOL:MCOL + 1],
                                       SC[:, MXP + 2:MXP + 3], ALU.max),
                                 after=[ecomb], inc="sV")
                    aftc = [ecomb]
                    aftc.append(ev_ctxe[h - 1] if h >= 1 else
                                (st[g - 1]["ctxe_all"] if g >= 1 else None))
                    e2 = P.op("tensor",
                              mk_tr(P_C[0:1, 264:392], SC[:, MCOL:MCOL + 1], IDF[:, :]),
                              after=aftc, inc="sP")
                    e3 = P.op("vector", mk_red(SC[0:1, MG:MG + 1], P_C[0:1, 264:392],
                                               ALU.max),
                              after=[e2], inc="sV")
                    e4 = P.op("tensor",
                              mk_mm(P_C[:, 392:393], ONE1[0:1, :], SC[0:1, MG:MG + 1],
                                    True, True),
                              after=[e3], inc="sP")
                    e5 = P.op("vector",
                              mk_ts(SC[:, EB:EB + 1], P_C[:, 392:393], -0.125, None,
                                    ALU.mult),
                              after=[e4], inc="sV")
                    aft = [e5]
                    aft.append(ev_cmm[h - 2] if h >= 2 else ptbuf_free[h % 2])
                    ev_exp[h] = P.op("scalar",
                                     mk_act(pt, P_S[:, :], AF.Exp,
                                            bias=SC[:, EB:EB + 1], scale=0.125),
                                     after=aft, inc="sA")
                    evc = None
                    for qc in range(TC):
                        for kc in range(KC):
                            aft = [ev_exp[h], ev_ve[kc]]
                            evc = P.op("tensor",
                                       mk_mm(P_C[:, qc * 66: qc * 66 + 66],
                                             pt[:, kc * 512 + qc * 128: kc * 512 + qc * 128 + 128],
                                             VA[:, kc * 264 + h * 66: kc * 264 + h * 66 + 66],
                                             kc == 0, kc == KC - 1),
                                       after=aft,
                                       inc="sP" if (qc == TC - 1 and kc == KC - 1) else None)
                    ev_cmm[h] = evc
                    ptbuf_free[h % 2] = evc
                    pc66 = P_C[:, 0:264].rearrange("p (qc w) -> p qc w", w=66)
                    e6 = P.op("vector",
                              lambda e, d=SC[:, RR4:RR4 + 4].rearrange("p (a b) -> p a b", b=1),
                              s_=pc66[:, :, 64:65]: e.reciprocal(d, s_),
                              after=[evc], inc="sV")
                    eve = None
                    for qc in range(TC):
                        col = qc * 1024 + r * 64 + 256 * h
                        eve = P.op("vector",
                                   mk_stt(Ag[:, col: col + 64],
                                          P_C[:, qc * 66: qc * 66 + 64],
                                          SC[:, RR4 + qc: RR4 + qc + 1],
                                          Ag[:, col: col + 64],
                                          ALU.mult, ALU.add),
                                   after=[e6], inc="sV" if qc == TC - 1 else None)
                    ev_ctxe[h] = eve
                    for (gg, ii) in ye_defer:
                        emit_ye(gg, ii)
                    if g == 1 and h < 2:
                        emit_peel_stripe(h + 6)
                        P.phase = "attn g%d h%d" % (g, h)
                ps_free = ev_exp[NH - 1]
                s["cmm_all"] = ev_cmm[NH - 1]
                s["ctxe_all"] = ev_ctxe[NH - 1]
                s["exp"] = ev_exp

            def ln_stats(sum_c, ssq_c, m_c, r_c, nmr_c, aft):
                ea = P.op("vector",
                          mk_ts(SC[:, m_c:m_c + 4], SC[:, sum_c:sum_c + 4],
                                1.0 / 1024.0, None, ALU.mult),
                          after=aft, inc="sV")
                eb_ = P.op("vector",
                           mk_ts(SC[:, VART:VART + 4], SC[:, ssq_c:ssq_c + 4],
                                 1.0 / 1024.0, None, ALU.mult),
                           after=aft, inc="sV")
                ec = P.op("vector",
                          mk_tt(SC[:, LNT:LNT + 4], SC[:, m_c:m_c + 4],
                                SC[:, m_c:m_c + 4], ALU.mult),
                          after=[ea], inc="sV")
                ed = P.op("vector",
                          mk_tt(SC[:, VART:VART + 4], SC[:, VART:VART + 4],
                                SC[:, LNT:LNT + 4], ALU.subtract),
                          after=[eb_, ec], inc="sV")
                ed = P.op("vector",
                          mk_ts(SC[:, VART:VART + 4], SC[:, VART:VART + 4],
                                0.0, None, ALU.max),
                          after=[ed], inc="sV")
                ee = P.op("scalar",
                          mk_act(SC[:, LNT:LNT + 4], SC[:, VART:VART + 4], AF.Ln,
                                 bias=SC[:, 63:64]),
                          after=[ed, ev_eps], inc="sA")
                ef = P.op("scalar",
                          mk_act(SC[:, r_c:r_c + 4], SC[:, LNT:LNT + 4], AF.Exp,
                                 scale=-0.5),
                          after=[ee], inc="sA")
                eg = P.op("vector",
                          mk_stt(SC[:, nmr_c:nmr_c + 4], SC[:, m_c:m_c + 4], -1.0,
                                 SC[:, r_c:r_c + 4], ALU.mult, ALU.mult),
                          after=[ef], inc="sV")
                return eg

            def emit_ap1(g):
                s = st[g]
                Ag = A[g % 3]
                ag_tile = [Ag[:, tc * 1024:(tc + 1) * 1024] for tc in range(TC)]
                ev_ap1 = [None] * TC
                for tc in range(TC):
                    aft = [s["st1"]]
                    if g >= 1:
                        aft += [st[g - 1]["ye"][7] if st[g - 1]["ye"][7] else None,
                                ev_store[g - 1]]
                    ev_ap1[tc] = P.op("vector",
                                      mk_ts(B[:, tc * 1024:(tc + 1) * 1024], ag_tile[tc],
                                            SC[:, R1 + tc: R1 + tc + 1],
                                            SC[:, NMR1 + tc: NMR1 + tc + 1],
                                            ALU.mult, ALU.add),
                                      after=aft, inc="sV")
                s["ap1"] = ev_ap1

            def phase_LN1(g, defer_ap1=False):
                P.phase = "LN1 g%d" % g
                r = g % 4
                Ag = A[g % 3]
                s = st[g]
                ag_tile = [Ag[:, tc * 1024:(tc + 1) * 1024] for tc in range(TC)]
                ev_redv = None
                for tc in (0, 1):
                    ev_redv = P.op("vector",
                                   mk_red(SC[:, ZS + tc: ZS + tc + 1], ag_tile[tc], ALU.add),
                                   after=[s["ctxe_all"]],
                                   inc="sV" if tc == 1 else None)
                for tc in (2, 3):
                    P.op("scalar",
                         mk_act(DSCA[:, :], ag_tile[tc], AF.Identity,
                                accum=SC[:, ZS + tc: ZS + tc + 1]),
                         after=[s["ctxe_all"]], inc=None)
                for tc in (0, 1):
                    P.op("scalar",
                         mk_act(DSCA[:, :], ag_tile[tc], AF.Square,
                                accum=SC[:, SSQ1 + tc: SSQ1 + tc + 1]),
                         after=[s["ctxe_all"]], inc=None)
                ev_sqa = P.op("scalar",
                              mk_act(SC[0:1, 67:68], SC[0:1, 67:68], AF.Identity),
                              after=[], inc="sA")
                ev_sqv = None
                for tc in (2, 3):
                    ev_sqv = P.op("vector",
                                  mk_stt(DSCV[:, :], ag_tile[tc], 1.0, ag_tile[tc],
                                         ALU.mult, ALU.mult,
                                         accum=SC[:, SSQ1 + tc: SSQ1 + tc + 1]),
                                  after=[s["ctxe_all"]],
                                  inc="sV" if tc == 3 else None)
                s["sq1"] = ev_sqa
                s["red_all"] = (ev_redv, ev_sqa, ev_sqv)
                ev_st1 = ln_stats(ZS, SSQ1, M1, R1, NMR1, [ev_redv, ev_sqa, ev_sqv])
                s["st1"] = ev_st1
                ev_x1bf = [None] * TC
                for tc in range(TC):
                    aft2 = [ev_st1]
                    if g >= 1:
                        aft2.append(st[g - 1]["T2done"])
                    ev_x1bf[tc] = P.op("scalar",
                                       mk_act(C2[:, tc * 1024:(tc + 1) * 1024], ag_tile[tc],
                                              AF.Identity,
                                              bias=SC[:, NMR1 + tc: NMR1 + tc + 1],
                                              scale=SC[:, R1 + tc: R1 + tc + 1]),
                                       after=aft2, inc="sA")
                s["x1bf"] = ev_x1bf
                if not defer_ap1:
                    emit_ap1(g)

            def phase_LN2_store(g):
                P.phase = "LN2 g%d" % g
                s = st[g]
                Ag = A[g % 3]
                ag_tile = [Ag[:, tc * 1024:(tc + 1) * 1024] for tc in range(TC)]
                z2h = SC[:, Z2H:Z2H + 8].rearrange("p (tc f) -> p tc f", f=2)
                e1 = P.op("vector",
                          mk_tt(SC[:, Z2S:Z2S + 4].rearrange("p (a b) -> p a b", b=1),
                                z2h[:, :, 0:1], z2h[:, :, 1:2], ALU.add),
                          after=[s["ye"][7]], inc="sV")
                ev_sq2 = None
                for tc in range(TC):
                    ev_sq2 = P.op("scalar",
                                  mk_act(DSCA[:, :], ag_tile[tc], AF.Square,
                                         accum=SC[:, SSQ2 + tc: SSQ2 + tc + 1]),
                                  after=[s["ye"][tc * 2 + 1]],
                                  inc="sA" if tc == TC - 1 else None)
                ev_sq2 = P.op("scalar",
                              mk_act(SC[0:1, 67:68], SC[0:1, 67:68], AF.Identity),
                              after=[ev_sq2], inc="sA")
                s["sq2"] = ev_sq2
                ev_st2 = ln_stats(Z2S, SSQ2, M1, R2, NMR2, [e1, ev_sq2])
                ev_ap2 = None
                for tc in range(TC):
                    ev_ap2 = P.op("vector",
                                  mk_ts(B[:, tc * 1024:(tc + 1) * 1024], ag_tile[tc],
                                        SC[:, R2 + tc: R2 + tc + 1],
                                        SC[:, NMR2 + tc: NMR2 + tc + 1],
                                        ALU.mult, ALU.add),
                                  after=[ev_st2, s["ye"][7]], inc="sV")
                ev_afree[g] = ev_ap2
                ev_store[g] = P.op("sync", dma(xstore_ap(g), t3(B)),
                                   after=[ev_ap2], inc="sQ", inc_amt=16)

            st[0]["xbf"] = ev_xbf0
            for g in range(NG):
                if g + 1 < NG and g >= 2:
                    ev_load[g + 1] = P.op("sync", dma(t3(A[(g + 1) % 3]), xload_ap(g + 1)),
                                          after=[ev_afree[g - 2]],
                                          inc="sQ", inc_amt=16)
                phase_T1_QKV(g)
                if g + 1 < NG and g != 1:
                    emit_xbf(g + 1)
                if g == 0:
                    emit_quant_rd2()
                if g >= 1:
                    phase_LN1(g - 1)
                if g >= 1:
                    phase_T2_FFN_front(g - 1)
                if g == 1:
                    emit_peel_stripe(4)
                emit_v_evacs(g)
                if g == 1:
                    emit_peel_stripe(5)
                phase_attention(g, g - 1 if g >= 1 else None)
                if g == 0:
                    emit_quant_scale()
                if g == 1:
                    ev_load[2] = P.op("sync", dma(t3(A[2]), xload_ap(2)),
                                      after=[qst["e3p"][4], qst["e3p"][5],
                                             qst["e3p"][6], qst["e3p"][7],
                                             qst["fin"][6], qst["fin"][7]],
                                      inc="sQ", inc_amt=16)
                    emit_xbf(2)
                if g >= 1:
                    emit_y_chains(g - 1, 8)
                    phase_LN2_store(g - 1)
                if g == 0:
                    for _s in range(4):
                        emit_peel_stripe(_s)
            phase_LN1(NG - 1)
            phase_T2_FFN_front(NG - 1)
            emit_y_chains(NG - 1, 8)
            phase_LN2_store(NG - 1)

            P.op("sync", lambda e: e.nop(), after=[ev_store[NG - 1]])
            P.emit(block)

    return nc


_LAST_PROG = None
_NC_CACHE = None


def make_in_maps(L0_attn_X, L0_attn_Wq, L0_attn_bq, L0_attn_Wkv, L0_attn_bkv,
                 L0_ffn_kernel, L0_ffn_bias, L0_ffn_a, L0_ffn_b):
    X = np.ascontiguousarray(np.asarray(L0_attn_X, dtype=np.float32)).reshape(32768, 1024)
    def _wperm(w):
        return np.ascontiguousarray(
            np.asarray(w, dtype=np.float32).reshape(8, 128, 4, 4, 64)
            .transpose(1, 3, 0, 2, 4).reshape(128, 8192))

    wkv_f = np.asarray(L0_attn_Wkv, dtype=np.float32)
    wkvb_f = np.asarray(L0_attn_bkv, dtype=np.float32)

    def _bcols(b):
        b4 = np.asarray(b, dtype=np.float32).reshape(4, 4, 64)
        cols = [b4[2 * cc:2 * cc + 2, r, :].reshape(128)
                for r in range(4) for cc in range(2)]
        return np.ascontiguousarray(np.stack(cols, axis=1))

    common = {
        "wqs": _wperm(L0_attn_Wq),
        "wks": _wperm(wkv_f[:, 0:1024]),
        "wvs": _wperm(wkv_f[:, 1024:2048]),
        "bqc": _bcols(L0_attn_bq),
        "bkc": _bcols(wkvb_f[0:1024]),
        "bvb": np.ascontiguousarray(
            np.tile(wkvb_f[1024:2048].reshape(4, 4, 64).transpose(1, 0, 2)
                    .reshape(1, 1024), (128, 1))),
        "fk": np.ascontiguousarray(L0_ffn_kernel, dtype=np.float32),
        "fbias": np.ascontiguousarray(L0_ffn_bias, dtype=np.float32),
        "fa": np.ascontiguousarray(L0_ffn_a, dtype=np.float32),
        "fb": np.ascontiguousarray(L0_ffn_b, dtype=np.float32),
        "identb": np.eye(128, dtype=ml_dtypes.bfloat16),
        "identf": np.eye(128, dtype=np.float32),
    }
    in_maps = []
    for c in range(NCORE):
        m = dict(common)
        m["xin"] = np.ascontiguousarray(X[c * 4096:(c + 1) * 4096, :])
        in_maps.append(m)
    return in_maps


def kernel(**inputs):
    global _NC_CACHE
    in_maps = make_in_maps(**inputs)
    if _NC_CACHE is None:
        _NC_CACHE = build_nc()
    res = run_bass_kernel_spmd(_NC_CACHE, in_maps, core_ids=list(range(NCORE)))
    out = np.concatenate([r["out"] for r in res.results], axis=0)
    return out.reshape(4, 8192, 1024)
